# version 01 (final)
# baseline (speedup 1.0000x reference)
import os
import numpy as np
import ml_dtypes
from contextlib import ExitStack

import concourse.bass as bass
import concourse.mybir as mybir
from concourse.bass_utils import run_bass_kernel_spmd

F32 = mybir.dt.float32
BF16 = mybir.dt.bfloat16
I32 = mybir.dt.int32
AF = mybir.ActivationFunctionType
ALU = mybir.AluOpType
BFNP = ml_dtypes.bfloat16

H = 2048
HEADS = 8
C = 256
DH = 256
JH = 2
DHH = 2
INTER = 5504
ICH = INTER // 128
NFC = H // 128
TPC = 4096
NT = TPC // 128
NCH = TPC // C
NTG = TPC // 512
EPS = 1e-5
NCORES = 8


def _gammas():
    betas = np.linspace(np.log2(1.0 / 32.0), np.log2(1.0 / 512.0), HEADS)
    return 1.0 - np.exp2(betas)


_CONST_CACHE = {}


def _consts():
    if _CONST_CACHE:
        return _CONST_CACHE
    g = _gammas().astype(np.float64)
    i = np.arange(C)
    dmask = np.zeros((128, HEADS * JH * C), dtype=np.float64)
    for h in range(HEADS):
        D = np.where(i[:, None] >= i[None, :], g[h] ** (i[:, None] - i[None, :]), 0.0)
        DT = D.T
        for jh in range(JH):
            dmask[:, (h * JH + jh) * C:(h * JH + jh + 1) * C] = DT[jh * 128:(jh + 1) * 128, :]
    crossm = np.zeros((128, HEADS * C), dtype=np.float64)
    for h in range(HEADS):
        crossm[:, h * C:(h + 1) * C] = (g[h] ** (i + 1))[None, :]
    kvs = np.zeros((128, HEADS * JH), dtype=np.float64)
    for h in range(HEADS):
        for jh in range(JH):
            j = jh * 128 + np.arange(128)
            kvs[:, h * JH + jh] = g[h] ** (C - 1 - j)
    _CONST_CACHE.update(
        dmask=dmask.astype(BFNP),
        crossm=crossm.astype(BFNP),
        kvs=kvs.astype(np.float32),
        ident=np.eye(128, dtype=BFNP),
        ones1=np.ones((1, 128), dtype=np.float32),
        chunk_decay=[float(g[h] ** C) for h in range(HEADS)],
    )
    return _CONST_CACHE


class Ctr:

    __slots__ = ("sem", "n")

    def __init__(self, sem):
        self.sem = sem
        self.n = 0

    def inc(self, amt=1):
        self.n += amt
        return self.sem, amt


def build_program(debug_stage=None):
    cst = _consts()
    CD = cst["chunk_decay"]
    nc = bass.Bass("TRN2", target_bir_lowering=False)

    x_in = nc.dram_tensor("x", [TPC, H], F32, kind="ExternalInput").ap()
    wq_in = nc.dram_tensor("wq", [H, H], F32, kind="ExternalInput").ap()
    wk_in = nc.dram_tensor("wk", [H, H], F32, kind="ExternalInput").ap()
    wv_in = nc.dram_tensor("wv", [H, H], F32, kind="ExternalInput").ap()
    bqc_in = nc.dram_tensor("bqc", [128, NFC], F32, kind="ExternalInput").ap()
    bkc_in = nc.dram_tensor("bkc", [128, NFC], F32, kind="ExternalInput").ap()
    bvbc_in = nc.dram_tensor("bvbc", [128, H], F32, kind="ExternalInput").ap()
    lutbc_in = nc.dram_tensor("lutbc", [128, 16], F32, kind="ExternalInput").ap()
    gidxT_in = nc.dram_tensor("gidxT", [H // 8, INTER], I32, kind="ExternalInput").ap()
    uidxT_in = nc.dram_tensor("uidxT", [H // 8, INTER], I32, kind="ExternalInput").ap()
    didxT_in = nc.dram_tensor("didxT", [H // 8, INTER], I32, kind="ExternalInput").ap()
    dmask_in = nc.dram_tensor("dmask", [128, HEADS * JH * C], BF16, kind="ExternalInput").ap()
    crossm_in = nc.dram_tensor("crossm", [128, HEADS * C], BF16, kind="ExternalInput").ap()
    kvs_in = nc.dram_tensor("kvs", [128, HEADS * JH], F32, kind="ExternalInput").ap()
    ident_in = nc.dram_tensor("ident", [128, 128], BF16, kind="ExternalInput").ap()
    ones_in = nc.dram_tensor("ones1", [1, 128], F32, kind="ExternalInput").ap()
    smask_in = nc.dram_tensor("smask", [128, 1], F32, kind="ExternalInput").ap()
    out_d = nc.dram_tensor("out", [TPC, H], F32, kind="ExternalOutput").ap()

    wgt_sh = nc.dram_tensor("wgt_sh", [H // 8, INTER], BF16)
    wut_sh = nc.dram_tensor("wut_sh", [H // 8, INTER], BF16)
    wdt_sh = nc.dram_tensor("wdt_sh", [H // 8, INTER], BF16)
    wgt_g = nc.dram_tensor("wgt_g", [H, INTER], BF16)
    wqbf_d = nc.dram_tensor("wqbf_d", [H, H], BF16)
    wkbf_d = nc.dram_tensor("wkbf_d", [H, H], BF16)
    wvbf_d = nc.dram_tensor("wvbf_d", [H, H], BF16)
    wut_g = nc.dram_tensor("wut_g", [H, INTER], BF16)
    wdt_g = nc.dram_tensor("wdt_g", [H, INTER], BF16)
    qf_d = nc.dram_tensor("qf_d", [H, TPC], BF16)
    kf_d = nc.dram_tensor("kf_d", [H, TPC], BF16)
    kt_d = nc.dram_tensor("kt_d", [TPC, H], BF16)
    vt_d = nc.dram_tensor("vt_d", [TPC, H], BF16)
    h_d = nc.dram_tensor("h_d", [INTER, TPC], BF16)
    s_d = nc.dram_tensor("s_d", [TPC, H], BF16)
    send_d = nc.dram_tensor("send_d", [128, HEADS * DHH * DH], F32)
    sgath_d = nc.dram_tensor("sgath_d", [256, HEADS * DHH * DH], F32)

    dbg = {}
    if debug_stage == "pd":
        dbg["wgt"] = nc.dram_tensor("dbg_wgt", [H, INTER], BF16, kind="ExternalOutput").ap()
        dbg["wdt"] = nc.dram_tensor("dbg_wdt", [H, INTER], BF16, kind="ExternalOutput").ap()
        dbg["wut"] = nc.dram_tensor("dbg_wut", [H, INTER], BF16, kind="ExternalOutput").ap()
        dbg["wgs"] = nc.dram_tensor("dbg_wgs", [H // 8, INTER], BF16, kind="ExternalOutput").ap()
        dbg["wds"] = nc.dram_tensor("dbg_wds", [H // 8, INTER], BF16, kind="ExternalOutput").ap()
    if debug_stage == "a0":
        dbg["xnt"] = nc.dram_tensor("dbg_xnt", [H, TPC], BF16, kind="ExternalOutput").ap()
    if debug_stage == "a1":
        dbg["qf"] = nc.dram_tensor("dbg_qf", [H, TPC], BF16, kind="ExternalOutput").ap()
        dbg["kt"] = nc.dram_tensor("dbg_kt", [TPC, H], BF16, kind="ExternalOutput").ap()
        dbg["vt"] = nc.dram_tensor("dbg_vt", [TPC, H], BF16, kind="ExternalOutput").ap()
    if debug_stage == "a1q":
        dbg["qf"] = nc.dram_tensor("dbg_qf", [H, TPC], BF16, kind="ExternalOutput").ap()
        dbg["kt"] = nc.dram_tensor("dbg_kt", [H, TPC], BF16, kind="ExternalOutput").ap()
        dbg["vt"] = nc.dram_tensor("dbg_vt", [TPC, H], BF16, kind="ExternalOutput").ap()
    if debug_stage == "sp":
        dbg["s0"] = nc.dram_tensor("dbg_s0", [128, HEADS * DHH * DH], F32, kind="ExternalOutput").ap()
    if debug_stage == "mix":
        dbg["s"] = nc.dram_tensor("dbg_s", [TPC, H], BF16, kind="ExternalOutput").ap()
    if debug_stage == "b1":
        dbg["h"] = nc.dram_tensor("dbg_h", [INTER, TPC], BF16, kind="ExternalOutput").ap()

    def dbg_block(pairs):
        with nc.Block() as blk:
            @blk.gpsimd
            def _(gp):
                n = 0
                for dst, src in pairs:
                    rows = dst.shape[0]
                    step = 512
                    for r0 in range(0, rows, step):
                        r1 = min(rows, r0 + step)
                        gp.dma_start(out=dst[r0:r1], in_=src[r0:r1]).then_inc(dbsem.sem, 16)
                        n += 16
                gp.wait_ge(dbsem.sem, n)

    es = ExitStack()
    sbuf = lambda name, shape, dt: es.enter_context(nc.sbuf_tensor(name, shape, dt))
    sem = lambda name: Ctr(es.enter_context(nc.semaphore(name)))
    dbsem = sem("dbsem")

    kvs_sb = sbuf("kvs_sb", [128, HEADS * JH], F32)
    ident_sb = sbuf("ident_sb", [128, 128], BF16)
    ones_sb = sbuf("ones_sb", [1, 128], F32)
    bqc_sb = sbuf("bqc_sb", [128, NFC], F32)
    bkc_sb = sbuf("bkc_sb", [128, NFC], F32)
    smask_sb = sbuf("smask_sb", [128, 1], F32)
    lutbc = sbuf("lutbc_sb", [128, 16], F32)

    with nc.Block() as blk:
        p0 = sem("p0")

        @blk.sync
        def _(sync):
            sync.dma_start(out=kvs_sb[:], in_=kvs_in).then_inc(*p0.inc(16))
            sync.dma_start(out=ident_sb[:], in_=ident_in).then_inc(*p0.inc(16))
            sync.dma_start(out=ones_sb[:], in_=ones_in).then_inc(*p0.inc(16))
            sync.dma_start(out=bqc_sb[:], in_=bqc_in).then_inc(*p0.inc(16))
            sync.dma_start(out=bkc_sb[:], in_=bkc_in).then_inc(*p0.inc(16))
            sync.dma_start(out=lutbc[:], in_=lutbc_in).then_inc(*p0.inc(16))
            sync.dma_start(out=smask_sb[:], in_=smask_in).then_inc(*p0.inc(16))

    pd_tiles = []
    for srct, dstt in ((gidxT_in, wgt_sh), (uidxT_in, wut_sh), (didxT_in, wdt_sh)):
        for t in range(2):
            pd_tiles.append((srct, dstt, t * 128, 128, INTER))

    cc_g = sem("cc_g")
    cc_u = sem("cc_u")
    cc_d = sem("cc_d")
    ph = ExitStack()
    pd_idx = [ph.enter_context(nc.sbuf_tensor(f"pd_idx{b}", [128, INTER], I32)) for b in range(2)]
    pd_idxb = [ph.enter_context(nc.sbuf_tensor(f"pd_idxb{b}", [128, INTER], BF16)) for b in range(2)]
    pd_dec = [ph.enter_context(nc.sbuf_tensor(f"pd_dec{b}", [128, INTER], BF16)) for b in range(2)]
    pd_term = [ph.enter_context(nc.sbuf_tensor(f"pd_term{b}", [128, INTER], BF16)) for b in range(2)]
    pd_red = ph.enter_context(nc.sbuf_tensor("pd_red", [128, 1], BF16))
    wc_f = [ph.enter_context(nc.sbuf_tensor(f"wc_f{b}", [128, H], F32)) for b in range(2)]
    wc_b = [ph.enter_context(nc.sbuf_tensor(f"wc_b{b}", [128, H], BF16)) for b in range(2)]
    with nc.Block() as blk:
        pd_in = [sem("pd_in0"), sem("pd_in1")]
        wc_in = [sem("wc_in0"), sem("wc_in1")]
        wc_ct = sem("wc_ct")
        wc_out = [sem("wc_out0"), sem("wc_out1")]
        pd_cast = sem("pd_cast")
        pd_dc = sem("pd_dc")
        pd_out = [sem("pd_out0"), sem("pd_out1")]

        in_marks = []

        for _ti in range(len(pd_tiles)):
            in_marks.append((pd_in[_ti % 2].sem, 16 * (_ti // 2 + 1)))

        @blk.sync
        def _(sync):
            for ti, (src, dst, r0, nr, ncol) in enumerate(pd_tiles):
                b = ti % 2
                if ti >= 2:
                    sync.wait_ge(pd_cast.sem, ti - 1)
                sync.dma_start(out=pd_idx[b][0:nr, 0:ncol],
                               in_=src[r0:r0 + nr, :]).then_inc(*pd_in[b].inc(16))

        dec_marks = []
        out_marks = {}
        for _ti, _t in enumerate(pd_tiles):
            _b = _ti % 2
            out_marks[_ti] = (pd_out[_b].sem, 16 * (_ti // 2 + 1))

        @blk.vector
        def _(vector):
            vector.wait_ge(p0.sem, p0.n)
            for ti, (src, dst, r0, nr, ncol) in enumerate(pd_tiles):
                b = ti % 2
                vector.wait_ge(*in_marks[ti])
                if ncol == INTER:
                    acc = pd_dec[b][0:nr, :]
                    idxb = pd_idxb[b][0:nr, :]
                    term = pd_term[b][0:nr, :]
                    srcb = pd_idx[b][0:nr, :]
                else:
                    acc = pd_decd[b][0:nr, :]
                    idxb = pd_idxbd[b][0:nr, :]
                    term = pd_termd[b][0:nr, :]
                    srcb = pd_idxd[b][0:nr, :]
                vector.tensor_copy(idxb, srcb).then_inc(*pd_cast.inc())
                if ti >= 2:
                    vector.wait_ge(*out_marks[ti - 2])
                    if os.environ.get("PD_DUMP_DEC") or os.environ.get("PD_DUMP_IX"):
                        ndump = sum(1 for j in range(ti - 1) if pd_tiles[j][4] == H) *                             (16 * (2 if os.environ.get("PD_DUMP_DEC") and os.environ.get("PD_DUMP_IX") else 1))
                        vector.wait_ge(dbsem.sem, ndump)
                three_op = bool(os.environ.get("PD_THREE_OP"))
                for l in range(16):
                    tgt = acc if l == 0 else term
                    if three_op:
                        vector.tensor_single_scalar(tgt, idxb, float(l), ALU.is_equal)
                        vector.tensor_scalar(tgt, tgt, lutbc[0:nr, l:l + 1], None, ALU.mult)
                    else:
                        vector.tensor_scalar(tgt, idxb, float(l), lutbc[0:nr, l:l + 1], ALU.is_equal, ALU.mult)
                    if l > 0:
                        vector.tensor_tensor(acc, acc, term, ALU.add)
                vector.tensor_reduce(pd_red[0:nr, 0:1], acc, mybir.AxisListType.X,
                                     ALU.max).then_inc(*pd_dc.inc())
                dec_marks.append(pd_dc.n)

        @blk.scalar
        def _(scalar):
            wtiles = []
            for wsrc, wdst in ((wq_in, wqbf_d), (wk_in, wkbf_d), (wv_in, wvbf_d)):
                for t in range(NFC):
                    wtiles.append((wsrc, wdst, t * 128))
            for wi, (wsrc, wdst, r0) in enumerate(wtiles):
                b = wi % 2
                if wi >= 2:
                    scalar.wait_ge(wc_ct.sem, wi - 1)
                scalar.dma_start(out=wc_f[b][:], in_=wsrc[r0:r0 + 128, :]).then_inc(*wc_in[b].inc(16))
                scalar.wait_ge(wc_in[b].sem, wc_in[b].n)
                if wi >= 2:
                    scalar.wait_ge(wc_out[b].sem, 16 * (wi // 2))
                scalar.activation(out=wc_b[b][:], in_=wc_f[b][:], func=AF.Copy).then_inc(*wc_ct.inc())
                scalar.wait_ge(wc_ct.sem, wc_ct.n)
                scalar.dma_start(out=wdst.ap()[r0:r0 + 128, :], in_=wc_b[b][:]).then_inc(*wc_out[b].inc(16))
            scalar.wait_ge(wc_out[0].sem, wc_out[0].n)
            scalar.wait_ge(wc_out[1].sem, wc_out[1].n)

        @blk.gpsimd
        def _(gp):
            for ti, (src, dst, r0, nr, ncol) in enumerate(pd_tiles):
                b = ti % 2
                gp.wait_ge(pd_dc.sem, dec_marks[ti])
                gp.dma_start(out=dst[r0:r0 + nr, :],
                             in_=pd_dec[b][0:nr, :]).then_inc(*pd_out[b].inc(16))
            gp.wait_ge(pd_out[0].sem, pd_out[0].n)
            gp.wait_ge(pd_out[1].sem, pd_out[1].n)
            for sh_t, g_t, cc_t in ((wgt_sh, wgt_g, cc_g), (wut_sh, wut_g, cc_u),
                                    (wdt_sh, wdt_g, cc_d)):
                gp.collective_compute(
                    "AllGather", ALU.bypass,
                    replica_groups=[list(range(NCORES))],
                    ins=[sh_t.ap().opt()], outs=[g_t.ap().opt()],
                ).then_inc(cc_t.sem, 1)
            gp.wait_ge(cc_g.sem, 1)
            gp.wait_ge(cc_u.sem, 1)
            gp.wait_ge(cc_d.sem, 1)

    ph.close()
    if debug_stage == "pd":
        dbg_block([(dbg["wgt"], wgt_g.ap()), (dbg["wdt"], wdt_g.ap()),
                   (dbg["wut"], wut_g.ap()),
                   (dbg["wgs"], wgt_sh.ap()), (dbg["wds"], wdt_sh.ap())])
        es.close()
        return nc

    S_f32 = es.enter_context(nc.sbuf_tensor("S_f32", [128, HEADS * DHH * DH], F32, side="right"))
    S_bf = es.enter_context(nc.sbuf_tensor("S_bf", [128, HEADS * DHH * DH], BF16, side="right"))
    xnT_scope = ExitStack()
    xnT = xnT_scope.enter_context(nc.sbuf_tensor("xnT", [128, NFC * TPC], BF16))

    ph = ExitStack()
    a0_x = ph.enter_context(nc.sbuf_tensor("a0_x", [128, 2 * H], F32))
    a0_dump = ph.enter_context(nc.sbuf_tensor("a0_dump", [128, H], F32))
    a0_stats = ph.enter_context(nc.sbuf_tensor("a0_stats", [128, 4 * NT], F32))
    a0_xn = ph.enter_context(nc.sbuf_tensor("a0_xn", [128, 2 * H], BF16))
    a0_pt0 = ph.enter_context(nc.psum_tensor("a0_pt0", [128, 1024], BF16))
    a0_pt1 = ph.enter_context(nc.psum_tensor("a0_pt1", [128, 1024], BF16))
    with nc.Block() as blk:
        a0_in = [sem("a0_in0"), sem("a0_in1")]
        a0_st = sem("a0_st")
        a0_rc = sem("a0_rc")
        a0_sq = sem("a0_sq")
        a0_xnd = sem("a0_xnd")
        a0_tp = sem("a0_tp")
        a0_ev = sem("a0_ev")
        a0_ti = sem("a0_ti")
        a0_pt = [a0_pt0, a0_pt1]

        @blk.sync
        def _(sync):
            for t in range(NT):
                if t >= 2:
                    sync.wait_ge(a0_xnd.sem, t - 1)
                sync.dma_start(out=a0_x[:, (t % 2) * H:(t % 2 + 1) * H],
                               in_=x_in[t * 128:(t + 1) * 128, :]).then_inc(*a0_in[t % 2].inc(16))

        @blk.scalar
        def _(scalar):
            for t in range(NT):
                xb = a0_x[:, (t % 2) * H:(t % 2 + 1) * H]
                scalar.wait_ge(a0_in[t % 2].sem, 16 * (t // 2 + 1))
                scalar.activation(out=a0_dump[:], in_=xb, func=AF.Copy,
                                  accum_out=a0_stats[:, t:t + 1])
                scalar.activation(out=a0_dump[:], in_=xb, func=AF.Square,
                                  accum_out=a0_stats[:, NT + t:NT + t + 1]).then_inc(*a0_st.inc())
                if t >= 1:
                    scalar.wait_ge(a0_rc.sem, t)
                    scalar.activation(out=a0_stats[:, 3 * NT + t - 1:3 * NT + t],
                                      in_=a0_stats[:, 3 * NT + t - 1:3 * NT + t],
                                      func=AF.Sqrt).then_inc(*a0_sq.inc())
                if t >= 2:
                    for fc in range(NFC):
                        k = (t - 2) * NFC + fc
                        scalar.wait_ge(a0_tp.sem, k + 1)
                        scalar.activation(
                            out=xnT[:, fc * TPC + (t - 2) * 128:fc * TPC + (t - 1) * 128],
                            in_=a0_pt[k % 2][:, 0:128],
                            func=AF.Copy).then_inc(*a0_ev.inc())
            scalar.wait_ge(a0_rc.sem, NT)
            scalar.activation(out=a0_stats[:, 4 * NT - 1:4 * NT],
                              in_=a0_stats[:, 4 * NT - 1:4 * NT],
                              func=AF.Sqrt).then_inc(*a0_sq.inc())
            for t in (NT - 2, NT - 1):
                for fc in range(NFC):
                    k = t * NFC + fc
                    scalar.wait_ge(a0_tp.sem, k + 1)
                    scalar.activation(
                        out=xnT[:, fc * TPC + t * 128:fc * TPC + (t + 1) * 128],
                        in_=a0_pt[k % 2][:, 0:128],
                        func=AF.Copy).then_inc(*a0_ev.inc())

        @blk.vector
        def _(vector):
            for t in range(NT):
                cs = a0_stats[:, t:t + 1]
                cq = a0_stats[:, NT + t:NT + t + 1]
                cm = a0_stats[:, 2 * NT + t:2 * NT + t + 1]
                cr = a0_stats[:, 3 * NT + t:3 * NT + t + 1]
                vector.wait_ge(a0_st.sem, t + 1)
                vector.tensor_scalar(cm, cs, 1.0 / H, None, ALU.mult).then_inc(*a0_ti.inc())
                vector.tensor_scalar(cq, cq, 1.0 / H, EPS, ALU.mult, ALU.add).then_inc(*a0_ti.inc())
                vector.wait_ge(a0_ti.sem, a0_ti.n)
                vector.tensor_tensor(cr, cm, cm, ALU.mult).then_inc(*a0_ti.inc())
                vector.wait_ge(a0_ti.sem, a0_ti.n)
                vector.tensor_tensor(cq, cq, cr, ALU.subtract).then_inc(*a0_ti.inc())
                vector.wait_ge(a0_ti.sem, a0_ti.n)
                vector.reciprocal(cr, cq).then_inc(*a0_rc.inc())
                vector.wait_ge(a0_sq.sem, t + 1)
                if t >= 2:
                    vector.wait_ge(a0_tp.sem, 16 * (t - 1))
                xb = a0_x[:, (t % 2) * H:(t % 2 + 1) * H]
                vector.tensor_scalar(a0_xn[:, (t % 2) * H:(t % 2 + 1) * H], xb,
                                     cm, cr, ALU.subtract, ALU.mult).then_inc(*a0_xnd.inc())

        @blk.tensor
        def _(tensor):
            k = 0
            for t in range(NT):
                tensor.wait_ge(a0_xnd.sem, t + 1)
                for fc in range(NFC):
                    if k >= 2:
                        tensor.wait_ge(a0_ev.sem, k - 1)
                    tensor.transpose(a0_pt[k % 2][:, 0:128],
                                     a0_xn[:, (t % 2) * H + fc * 128:(t % 2) * H + (fc + 1) * 128],
                                     ident_sb[:]).then_inc(*a0_tp.inc())
                    k += 1

    ph.close()
    if debug_stage == "a0":
        with nc.Block() as blk:
            @blk.gpsimd
            def _(gp):
                n = 0
                for fc in range(NFC):
                    gp.dma_start(out=dbg["xnt"][fc * 128:(fc + 1) * 128, :],
                                 in_=xnT[:, fc * TPC:(fc + 1) * TPC]).then_inc(*dbsem.inc(16))
                gp.wait_ge(dbsem.sem, dbsem.n)
        xnT_scope.close()
        es.close()
        return nc

    ph = ExitStack()
    a1_wb = [ph.enter_context(nc.sbuf_tensor(f"a1_wb{b}", [128, H], BF16)) for b in range(2)]
    a1_wv = [ph.enter_context(nc.sbuf_tensor(f"a1_wv{b}", [128, NFC * 256], BF16)) for b in range(2)]
    a1_st = ph.enter_context(nc.sbuf_tensor("a1_st", [128, 4 * 512], BF16))
    a1_bvbc = ph.enter_context(nc.sbuf_tensor("a1_bvbc", [128, H], F32))
    a1_ones = ph.enter_context(nc.sbuf_tensor("a1_ones", [128, 512], F32))
    a1_bt = [ph.enter_context(nc.sbuf_tensor(f"a1_bt{b}", [128, 512], F32)) for b in range(2)]
    a1_ps = ph.enter_context(nc.psum_tensor("a1_ps", [128, 4 * 512], F32))
    with nc.Block() as blk:
        a1_w = [sem("a1_w0"), sem("a1_w1")]
        a1_v = [sem("a1_v0"), sem("a1_v1")]
        a1_bv = sem("a1_bv")
        a1_on = sem("a1_on")
        a1_mm = sem("a1_mm")
        a1_ev = sem("a1_ev")
        a1_o = [sem(f"a1_o{s}") for s in range(4)]

        qk_keys = [("q", ot) for ot in range(NFC)] + [("k", ot) for ot in range(NFC)]
        if os.environ.get("A1_NTILES"):
            qk_keys = qk_keys[:int(os.environ["A1_NTILES"])]
        NQK = len(qk_keys)
        NVOG = 0 if os.environ.get("A1_NOV") else 8

        @blk.sync
        def _(sync):
            sync.dma_start(out=a1_bvbc[:], in_=bvbc_in).then_inc(*a1_bv.inc(16))
            for wi, (proj, ot) in enumerate(qk_keys):
                wsrc = wqbf_d if proj == "q" else wkbf_d
                b = wi % 2
                if wi >= 2:
                    sync.wait_ge(a1_mm.sem, (wi - 1) * NTG)
                sync.dma_start(
                    out=a1_wb[b][:].rearrange("p (c o) -> p c o", o=128),
                    in_=wsrc.ap()[:, ot * 128:(ot + 1) * 128].rearrange("(c p) o -> p c o", p=128),
                ).then_inc(*a1_w[b].inc(16))
            for vi in range(NVOG):
                b = vi % 2
                if vi >= 2:
                    sync.wait_ge(a1_mm.sem, NQK * NTG + (vi - 1) * NT)
                sync.dma_start(
                    out=a1_wv[b][:].rearrange("p (c o) -> p c o", o=256),
                    in_=wvbf_d.ap()[:, vi * 256:(vi + 1) * 256].rearrange("(c p) o -> p c o", p=128),
                ).then_inc(*a1_v[b].inc(16))

        @blk.tensor
        def _(tensor):
            gi = 0
            for wi, (proj, ot) in enumerate(qk_keys):
                b = wi % 2
                tensor.wait_ge(a1_w[b].sem, 16 * (wi // 2 + 1))
                for tg in range(NTG):
                    pb = (gi % 4) * 512
                    if os.environ.get("A1_SERIAL") and gi >= 1:
                        tensor.wait_ge(a1_ev.sem, gi)
                    elif gi >= 4 and not os.environ.get("A1_STUB"):
                        tensor.wait_ge(a1_ev.sem, gi - 3)
                    for fc in range(NFC):
                        ins = tensor.matmul(
                            a1_ps[:, pb:pb + 512],
                            a1_wb[b][:, fc * 128:(fc + 1) * 128],
                            xnT[:, fc * TPC + tg * 512:fc * TPC + (tg + 1) * 512],
                            start=(fc == 0), stop=(fc == NFC - 1))
                    ins.then_inc(*a1_mm.inc())
                    gi += 1
            for vi in range(NVOG):
                b = vi % 2
                tensor.wait_ge(a1_v[b].sem, 16 * (vi // 2 + 1))
                for tt in range(NT):
                    pb = (gi % 4) * 512
                    if gi >= 4:
                        tensor.wait_ge(a1_ev.sem, gi - 3)
                    for fc in range(NFC):
                        ins = tensor.matmul(
                            a1_ps[:, pb:pb + 256],
                            xnT[:, fc * TPC + tt * 128:fc * TPC + (tt + 1) * 128],
                            a1_wv[b][:, fc * 256:(fc + 1) * 256],
                            start=(fc == 0), stop=(fc == NFC - 1))
                    ins.then_inc(*a1_mm.inc())
                    gi += 1

        @blk.vector
        def _(vector):
            if os.environ.get("A1_STUB"):
                return
            stub2 = bool(os.environ.get("A1_STUB2"))
            vecmin = os.environ.get("A1_VECMIN")
            gi = 0
            vector.wait_ge(a1_bv.sem, 16)
            vector.memset(a1_ones[:], 1.0).then_inc(*a1_on.inc())
            vector.wait_ge(a1_on.sem, 1)
            wi = 0
            for proj, bias in (("q", bqc_sb), ("k", bkc_sb)):
                for ot in range(NFC):
                    vector.tensor_scalar(a1_bt[wi % 2][:], a1_ones[:],
                                         bias[:, ot:ot + 1], None, ALU.mult)
                    wi += 1
                    for tg in range(NTG):
                        if vecmin == "1":
                            vector.nop().then_inc(*a1_ev.inc())
                            gi += 1
                            continue
                        if vecmin == "3":
                            vector.wait_ge(a1_mm.sem, gi + 1)
                            vector.nop().then_inc(*a1_ev.inc())
                            gi += 1
                            continue
                        if vecmin == "4":
                            vector.nop().then_inc(*a1_ev.inc())
                            gi += 1
                            continue
                        vector.wait_ge(a1_mm.sem, gi + 1)
                        if gi >= 4 and not stub2:
                            vector.wait_ge(a1_o[gi % 4].sem, 16 * (gi // 4))
                        pb = (gi % 4) * 512
                        srcp = a1_bt[(wi - 1) % 2][:] if vecmin == "2" else a1_ps[:, pb:pb + 512]
                        vector.tensor_tensor(a1_st[:, pb:pb + 512], srcp,
                                             a1_bt[(wi - 1) % 2][:],
                                             ALU.add).then_inc(*a1_ev.inc())
                        gi += 1
            for vi in range(NVOG):
                for tt in range(NT):
                    vector.wait_ge(a1_mm.sem, gi + 1)
                    if gi >= 4:
                        vector.wait_ge(a1_o[gi % 4].sem, 16 * (gi // 4))
                    pb = (gi % 4) * 512
                    vector.tensor_tensor(a1_st[:, pb:pb + 256], a1_ps[:, pb:pb + 256],
                                         a1_bvbc[:, vi * 256:(vi + 1) * 256],
                                         ALU.add).then_inc(*a1_ev.inc())
                    gi += 1

        @blk.gpsimd
        def _(gp):
            if os.environ.get("A1_STUB") or os.environ.get("A1_STUB2"):
                return
            gi = 0
            for proj, dst in (("q", qf_d), ("k", kf_d)):
                for ot in range(NFC):
                    for tg in range(NTG):
                        gp.wait_ge(a1_ev.sem, gi + 1)
                        pb = (gi % 4) * 512
                        gp.dma_start(out=dst[ot * 128:(ot + 1) * 128, tg * 512:(tg + 1) * 512],
                                     in_=a1_st[:, pb:pb + 512]).then_inc(*a1_o[gi % 4].inc(16))
                        gi += 1
            for vi in range(NVOG):
                for tt in range(NT):
                    gp.wait_ge(a1_ev.sem, gi + 1)
                    pb = (gi % 4) * 512
                    gp.dma_start(out=vt_d[tt * 128:(tt + 1) * 128, vi * 256:(vi + 1) * 256],
                                 in_=a1_st[:, pb:pb + 256]).then_inc(*a1_o[gi % 4].inc(16))
                    gi += 1
            for s in range(4):
                gp.wait_ge(a1_o[s].sem, a1_o[s].n)

    ph.close()

    if debug_stage == "a1q":
        dbg_block([(dbg["qf"], qf_d.ap()), (dbg["kt"], kf_d.ap()), (dbg["vt"], vt_d.ap())])
        xnT_scope.close()
        es.close()
        return nc

    ph = ExitStack()
    k_fm = ph.enter_context(nc.sbuf_tensor("k_fm", [128, 2 * 512], BF16))
    k_ts = ph.enter_context(nc.sbuf_tensor("k_ts", [128, 4 * 128], BF16))
    k_pt0 = ph.enter_context(nc.psum_tensor("k_pt0", [128, 1024], BF16))
    k_pt1 = ph.enter_context(nc.psum_tensor("k_pt1", [128, 1024], BF16))
    with nc.Block() as blk:
        kt_in = [sem("kt_in0"), sem("kt_in1")]
        kt_tp = sem("kt_tp")
        kt_ev = sem("kt_ev")
        kt_out = [sem(f"kt_out{s}") for s in range(4)]
        k_pt = [k_pt0, k_pt1]

        @blk.sync
        def _(sync):
            i = 0
            for ot in range(NFC):
                for tg in range(NTG):
                    if i >= 2:
                        sync.wait_ge(kt_tp.sem, 4 * (i - 1))
                    sync.dma_start(out=k_fm[:, (i % 2) * 512:(i % 2 + 1) * 512],
                                   in_=kf_d[ot * 128:(ot + 1) * 128, tg * 512:(tg + 1) * 512]
                                   ).then_inc(*kt_in[i % 2].inc(16))
                    i += 1

        @blk.tensor
        def _(tensor):
            i = 0
            for ot in range(NFC):
                for tg in range(NTG):
                    tensor.wait_ge(kt_in[i % 2].sem, 16 * (i // 2 + 1))
                    for tti in range(4):
                        k4 = i * 4 + tti
                        if k4 >= 2:
                            tensor.wait_ge(kt_ev.sem, k4 - 1)
                        tensor.transpose(k_pt[k4 % 2][:, 0:128],
                                         k_fm[:, (i % 2) * 512 + tti * 128:(i % 2) * 512 + (tti + 1) * 128],
                                         ident_sb[:]).then_inc(*kt_tp.inc())
                    i += 1

        @blk.vector
        def _(vector):
            k4 = 0
            for ot in range(NFC):
                for tg in range(NTG):
                    for tti in range(4):
                        vector.wait_ge(kt_tp.sem, k4 + 1)
                        if k4 >= 4:
                            vector.wait_ge(kt_out[k4 % 4].sem, 16 * (k4 // 4))
                        vector.tensor_copy(k_ts[:, (k4 % 4) * 128:(k4 % 4 + 1) * 128],
                                           k_pt[k4 % 2][:, 0:128]).then_inc(*kt_ev.inc())
                        k4 += 1

        @blk.gpsimd
        def _(gp):
            k4 = 0
            for ot in range(NFC):
                for tg in range(NTG):
                    for tti in range(4):
                        gp.wait_ge(kt_ev.sem, k4 + 1)
                        tt = tg * 4 + tti
                        gp.dma_start(out=kt_d[tt * 128:(tt + 1) * 128, ot * 128:(ot + 1) * 128],
                                     in_=k_ts[:, (k4 % 4) * 128:(k4 % 4 + 1) * 128]).then_inc(*kt_out[k4 % 4].inc(16))
                        k4 += 1
            for s in range(4):
                gp.wait_ge(kt_out[s].sem, kt_out[s].n)

    ph.close()
    if debug_stage == "a1":
        dbg_block([(dbg["qf"], qf_d.ap()), (dbg["kt"], kt_d.ap()), (dbg["vt"], vt_d.ap())])
        xnT_scope.close()
        es.close()
        return nc

    ph = ExitStack()
    sp_kt = ph.enter_context(nc.sbuf_tensor("sp_kt", [128, 2 * JH * H], BF16))
    sp_vt = ph.enter_context(nc.sbuf_tensor("sp_vt", [128, 2 * JH * H], BF16))
    sp_ktl = ph.enter_context(nc.sbuf_tensor("sp_ktl", [128, JH * H], BF16))
    sp_ps = ph.enter_context(nc.psum_tensor("sp_ps", [128, 2 * 512], F32))
    with nc.Block() as blk:
        sp_in = [sem("sp_in0"), sem("sp_in1")]
        sp_z = sem("sp_z")
        sp_ktd = sem("sp_ktd")
        sp_mm = sem("sp_mm")
        sp_up = sem("sp_up")
        sp_st = sem("sp_st")
        sp_cc = sem("sp_cc")
        sp_gb = sem("sp_gb")
        sp_s0 = sem("sp_s0")

        @blk.sync
        def _(sync):
            for n in range(NCH):
                b = n % 2
                if n >= 2:
                    sync.wait_ge(sp_up.sem, (n - 1) * HEADS)
                for jh in range(JH):
                    sync.dma_start(out=sp_kt[:, (b * JH + jh) * H:(b * JH + jh + 1) * H],
                                   in_=kt_d[n * C + jh * 128:n * C + (jh + 1) * 128, :]).then_inc(*sp_in[b].inc(16))
                    sync.dma_start(out=sp_vt[:, (b * JH + jh) * H:(b * JH + jh + 1) * H],
                                   in_=vt_d[n * C + jh * 128:n * C + (jh + 1) * 128, :]).then_inc(*sp_in[b].inc(16))

        @blk.vector
        def _(vector):
            vector.memzero(S_f32[:]).then_inc(*sp_z.inc())
            for n in range(NCH):
                b = n % 2
                vector.wait_ge(sp_in[n % 2].sem, 64 * (n // 2 + 1))
                if n >= 1:
                    vector.wait_ge(sp_mm.sem, n * HEADS)
                for h in range(HEADS):
                    for jh in range(JH):
                        ins = vector.tensor_scalar(
                            sp_ktl[:, jh * H + h * DH:jh * H + (h + 1) * DH],
                            sp_kt[:, (b * JH + jh) * H + h * DH:(b * JH + jh) * H + (h + 1) * DH],
                            kvs_sb[:, h * JH + jh:h * JH + jh + 1], None, ALU.mult)
                        if h == HEADS - 1 and jh == JH - 1:
                            ins.then_inc(*sp_ktd.inc())
                for h in range(HEADS):
                    vector.wait_ge(sp_mm.sem, n * HEADS + h + 1)
                    vector.tensor_scalar(S_f32[:, h * 512:(h + 1) * 512],
                                         S_f32[:, h * 512:(h + 1) * 512],
                                         CD[h], None, ALU.mult)
                    vector.tensor_tensor(S_f32[:, h * 512:(h + 1) * 512],
                                         S_f32[:, h * 512:(h + 1) * 512],
                                         sp_ps[:, (h % 2) * 512:(h % 2 + 1) * 512],
                                         ALU.add).then_inc(*sp_up.inc())
            vector.wait_ge(sp_gb.sem, 16)
            vector.tensor_scalar(S_f32[:], S_f32[:], smask_sb[:, 0:1], None,
                                 ALU.mult).then_inc(*sp_s0.inc())
            vector.tensor_copy(S_bf[:], S_f32[:]).then_inc(*sp_s0.inc())

        @blk.tensor
        def _(tensor):
            tensor.wait_ge(sp_z.sem, 1)
            for n in range(NCH):
                b = n % 2
                tensor.wait_ge(sp_ktd.sem, n + 1)
                for h in range(HEADS):
                    if n * HEADS + h >= 2:
                        tensor.wait_ge(sp_up.sem, n * HEADS + h - 1)
                    pb = (h % 2) * 512
                    for dh in range(DHH):
                        for jh in range(JH):
                            ins = tensor.matmul(
                                sp_ps[:, pb + dh * 256:pb + (dh + 1) * 256],
                                sp_ktl[:, jh * H + h * DH + dh * 128:jh * H + h * DH + (dh + 1) * 128],
                                sp_vt[:, (b * JH + jh) * H + h * DH:(b * JH + jh) * H + (h + 1) * DH],
                                start=(jh == 0), stop=(jh == JH - 1))
                    ins.then_inc(*sp_mm.inc())

        @blk.gpsimd
        def _(gp):
            gp.wait_ge(sp_up.sem, NCH * HEADS)
            gp.dma_start(out=send_d.ap(), in_=S_f32[:]).then_inc(*sp_st.inc(16))
            gp.wait_ge(sp_st.sem, 16)
            gp.collective_compute(
                "AllGather", ALU.bypass,
                replica_groups=[[0, 1], [2, 3], [4, 5], [6, 7]],
                ins=[send_d.ap().opt()], outs=[sgath_d.ap().opt()],
            ).then_inc(sp_cc.sem, 1)
            gp.wait_ge(sp_cc.sem, 1)
            gp.dma_start(out=S_f32[:], in_=sgath_d[0:128, :]).then_inc(*sp_gb.inc(16))
            gp.wait_ge(sp_s0.sem, 2)

    ph.close()
    if debug_stage == "sp":
        dbg_block([(dbg["s0"], send_d.ap())])
        xnT_scope.close()
        es.close()
        return nc

    ph = ExitStack()
    b1_w = ph.enter_context(nc.sbuf_tensor("b1_w", [128, 4 * H], BF16))
    b1_gs = ph.enter_context(nc.sbuf_tensor("b1_gs", [128, 2 * 512], BF16))
    b1_h = ph.enter_context(nc.sbuf_tensor("b1_h", [128, 2 * TPC], BF16))
    b1_ps = ph.enter_context(nc.psum_tensor("b1_ps", [128, 4 * 512], F32))
    with nc.Block() as blk:
        b1_win = [sem("b1_win0"), sem("b1_win1")]
        b1_mm = sem("b1_mm")
        b1_sil = sem("b1_sil")
        b1_hd = sem("b1_hd")
        b1_out = [sem("b1_out0"), sem("b1_out1")]

        @blk.sync
        def _(sync):
            sync.wait_ge(cc_g.sem, 1)
            sync.wait_ge(cc_u.sem, 1)
            for ic in range(ICH):
                b = ic % 2
                if ic >= 2:
                    sync.wait_ge(b1_mm.sem, (ic - 1) * 2 * NTG)
                sync.dma_start(
                    out=b1_w[:, (2 * b) * H:(2 * b + 1) * H].rearrange("p (c i) -> p c i", i=128),
                    in_=wgt_g.ap()[:, ic * 128:(ic + 1) * 128].rearrange("(c p) i -> p c i", p=128),
                ).then_inc(*b1_win[b].inc(16))
                sync.dma_start(
                    out=b1_w[:, (2 * b + 1) * H:(2 * b + 2) * H].rearrange("p (c i) -> p c i", i=128),
                    in_=wut_g.ap()[:, ic * 128:(ic + 1) * 128].rearrange("(c p) i -> p c i", p=128),
                ).then_inc(*b1_win[b].inc(16))

        @blk.tensor
        def _(tensor):
            gi = 0
            for ic in range(ICH):
                b = ic % 2
                tensor.wait_ge(b1_win[b].sem, 32 * (ic // 2 + 1))
                for tg in range(NTG):
                    pbg = (gi % 2) * 512
                    pbu = 1024 + (gi % 2) * 512
                    if gi >= 2:
                        tensor.wait_ge(b1_sil.sem, gi - 1)
                        tensor.wait_ge(b1_hd.sem, gi - 1)
                    for fc in range(NFC):
                        ins = tensor.matmul(
                            b1_ps[:, pbg:pbg + 512],
                            b1_w[:, (2 * b) * H + fc * 128:(2 * b) * H + (fc + 1) * 128],
                            xnT[:, fc * TPC + tg * 512:fc * TPC + (tg + 1) * 512],
                            start=(fc == 0), stop=(fc == NFC - 1))
                    ins.then_inc(*b1_mm.inc())
                    for fc in range(NFC):
                        ins = tensor.matmul(
                            b1_ps[:, pbu:pbu + 512],
                            b1_w[:, (2 * b + 1) * H + fc * 128:(2 * b + 1) * H + (fc + 1) * 128],
                            xnT[:, fc * TPC + tg * 512:fc * TPC + (tg + 1) * 512],
                            start=(fc == 0), stop=(fc == NFC - 1))
                    ins.then_inc(*b1_mm.inc())
                    gi += 1

        @blk.scalar
        def _(scalar):
            gi = 0
            for ic in range(ICH):
                for tg in range(NTG):
                    scalar.wait_ge(b1_mm.sem, gi * 2 + 1)
                    scalar.activation(out=b1_gs[:, (gi % 2) * 512:(gi % 2 + 1) * 512],
                                      in_=b1_ps[:, (gi % 2) * 512:(gi % 2 + 1) * 512],
                                      func=AF.Silu).then_inc(*b1_sil.inc())
                    gi += 1

        @blk.vector
        def _(vector):
            gi = 0
            for ic in range(ICH):
                hb = ic % 2
                if ic >= 2:
                    vector.wait_ge(b1_out[ic % 2].sem, 16 * (ic // 2))
                for tg in range(NTG):
                    vector.wait_ge(b1_mm.sem, gi * 2 + 2)
                    vector.wait_ge(b1_sil.sem, gi + 1)
                    vector.tensor_tensor(
                        b1_h[:, hb * TPC + tg * 512:hb * TPC + (tg + 1) * 512],
                        b1_ps[:, 1024 + (gi % 2) * 512:1024 + (gi % 2 + 1) * 512],
                        b1_gs[:, (gi % 2) * 512:(gi % 2 + 1) * 512],
                        ALU.mult).then_inc(*b1_hd.inc())
                    gi += 1

        @blk.gpsimd
        def _(gp):
            for ic in range(ICH):
                hb = ic % 2
                gp.wait_ge(b1_hd.sem, (ic + 1) * NTG)
                gp.dma_start(out=h_d[ic * 128:(ic + 1) * 128, :],
                             in_=b1_h[:, hb * TPC:(hb + 1) * TPC]).then_inc(*b1_out[hb].inc(16))
            gp.wait_ge(b1_out[0].sem, b1_out[0].n)
            gp.wait_ge(b1_out[1].sem, b1_out[1].n)

    ph.close()
    if debug_stage == "b1":
        dbg_block([(dbg["h"], h_d.ap())])
        xnT_scope.close()
        es.close()
        return nc

    xnT_scope.close()

    ph = ExitStack()
    mx_dm = ph.enter_context(nc.sbuf_tensor("mx_dm", [128, HEADS * JH * C], BF16))
    mx_cm = ph.enter_context(nc.sbuf_tensor("mx_cm", [128, HEADS * C], BF16))
    mx_q = ph.enter_context(nc.sbuf_tensor("mx_q", [128, 2 * TPC], BF16))
    mx_k = ph.enter_context(nc.sbuf_tensor("mx_k", [128, 2 * TPC], BF16))
    mx_kt = ph.enter_context(nc.sbuf_tensor("mx_kt", [128, 2 * JH * H], BF16))
    mx_vt = ph.enter_context(nc.sbuf_tensor("mx_vt", [128, 2 * JH * H], BF16))
    mx_qg = ph.enter_context(nc.sbuf_tensor("mx_qg", [128, 2 * TPC], BF16))
    mx_ktl = ph.enter_context(nc.sbuf_tensor("mx_ktl", [128, 2 * JH * H], BF16))
    mx_adT = ph.enter_context(nc.sbuf_tensor("mx_adT", [128, 2 * 512], BF16))
    mx_s = ph.enter_context(nc.sbuf_tensor("mx_s", [128, 2 * JH * H], BF16))
    mx_pa = ph.enter_context(nc.psum_tensor("mx_pa", [128, 2 * 512], F32))
    mx_po = ph.enter_context(nc.psum_tensor("mx_po", [128, 2 * 512], F32))
    mx_pk = ph.enter_context(nc.psum_tensor("mx_pk", [128, 2 * 512], F32))
    with nc.Block() as blk:
        mxc = sem("mxc")
        mx_in = [sem("mx_in0"), sem("mx_in1")]
        mx_pre = sem("mx_pre")
        mx_at = sem("mx_at")
        mx_me = sem("mx_me")
        mx_om = sem("mx_om")
        mx_km = sem("mx_km")
        mx_se = sem("mx_se")
        mx_su = sem("mx_su")
        mx_sb = sem("mx_sb")
        mx_sout = [sem("mx_sout0"), sem("mx_sout1")]

        @blk.sync
        def _(sync):
            sync.dma_start(out=mx_dm[:], in_=dmask_in).then_inc(*mxc.inc(16))
            sync.dma_start(out=mx_cm[:], in_=crossm_in).then_inc(*mxc.inc(16))
            for n in range(NCH):
                b = n % 2
                if n >= 2:
                    sync.wait_ge(mx_km.sem, (n - 1) * HEADS)
                    sync.wait_ge(mx_om.sem, (n - 1) * HEADS)
                sync.dma_start(out=mx_q[:, b * TPC:(b + 1) * TPC].rearrange("p (c t) -> p c t", t=C),
                               in_=qf_d.ap()[:, n * C:(n + 1) * C].rearrange("(c p) t -> p c t", p=128)
                               ).then_inc(*mx_in[b].inc(16))
                sync.dma_start(out=mx_k[:, b * TPC:(b + 1) * TPC].rearrange("p (c t) -> p c t", t=C),
                               in_=kf_d.ap()[:, n * C:(n + 1) * C].rearrange("(c p) t -> p c t", p=128)
                               ).then_inc(*mx_in[b].inc(16))
                for jh in range(JH):
                    sync.dma_start(out=mx_kt[:, (b * JH + jh) * H:(b * JH + jh + 1) * H],
                                   in_=kt_d[n * C + jh * 128:n * C + (jh + 1) * 128, :]).then_inc(*mx_in[b].inc(16))
                    sync.dma_start(out=mx_vt[:, (b * JH + jh) * H:(b * JH + jh + 1) * H],
                                   in_=vt_d[n * C + jh * 128:n * C + (jh + 1) * 128, :]).then_inc(*mx_in[b].inc(16))

        @blk.vector
        def _(vector):
            vector.wait_ge(mxc.sem, 32)
            for n in range(NCH):
                b = n % 2
                vector.wait_ge(mx_in[b].sem, 96 * (n // 2 + 1))
                if n >= 2:
                    vector.wait_ge(mx_om.sem, (n - 1) * HEADS)
                    vector.wait_ge(mx_km.sem, (n - 1) * HEADS)
                for h in range(HEADS):
                    for dc in range(DHH):
                        oc = 2 * h + dc
                        vector.tensor_tensor(
                            mx_qg[:, b * TPC + oc * C:b * TPC + (oc + 1) * C],
                            mx_q[:, b * TPC + oc * C:b * TPC + (oc + 1) * C],
                            mx_cm[:, h * C:(h + 1) * C], ALU.mult)
                for h in range(HEADS):
                    for jh in range(JH):
                        ins = vector.tensor_scalar(
                            mx_ktl[:, (b * JH + jh) * H + h * DH:(b * JH + jh) * H + (h + 1) * DH],
                            mx_kt[:, (b * JH + jh) * H + h * DH:(b * JH + jh) * H + (h + 1) * DH],
                            kvs_sb[:, h * JH + jh:h * JH + jh + 1], None, ALU.mult)
                        if h == HEADS - 1 and jh == JH - 1:
                            ins.then_inc(*mx_pre.inc())
                for h in range(HEADS):
                    hi = n * HEADS + h
                    vector.wait_ge(mx_at.sem, hi + 1)
                    if hi >= 2:
                        vector.wait_ge(mx_om.sem, hi - 1)
                    for jh in range(JH):
                        ins = vector.tensor_tensor(
                            mx_adT[:, (hi % 2) * 512 + jh * 256:(hi % 2) * 512 + (jh + 1) * 256],
                            mx_pa[:, (hi % 2) * 512 + jh * 256:(hi % 2) * 512 + (jh + 1) * 256],
                            mx_dm[:, (h * JH + jh) * C:(h * JH + jh + 1) * C],
                            ALU.mult)
                        if jh == JH - 1:
                            ins.then_inc(*mx_me.inc())
                    vector.wait_ge(mx_km.sem, hi + 1)
                    vector.tensor_scalar(S_f32[:, h * 512:(h + 1) * 512],
                                         S_f32[:, h * 512:(h + 1) * 512],
                                         CD[h], None, ALU.mult)
                    vector.tensor_tensor(S_f32[:, h * 512:(h + 1) * 512],
                                         S_f32[:, h * 512:(h + 1) * 512],
                                         mx_pk[:, (hi % 2) * 512:(hi % 2 + 1) * 512],
                                         ALU.add).then_inc(*mx_su.inc())
                    vector.wait_ge(mx_om.sem, hi + 1)
                    vector.wait_ge(mx_su.sem, hi + 1)
                    vector.tensor_copy(S_bf[:, h * 512:(h + 1) * 512],
                                       S_f32[:, h * 512:(h + 1) * 512]).then_inc(*mx_sb.inc())

        @blk.tensor
        def _(tensor):
            for n in range(NCH):
                b = n % 2
                tensor.wait_ge(mx_pre.sem, n + 1)
                for h in range(HEADS):
                    hi = n * HEADS + h
                    pb = (hi % 2) * 512
                    if hi >= 2:
                        tensor.wait_ge(mx_me.sem, hi - 1)
                    for jh in range(JH):
                        for dc in range(DHH):
                            oc = 2 * h + dc
                            ins = tensor.matmul(
                                mx_pa[:, pb + jh * 256:pb + (jh + 1) * 256],
                                mx_k[:, b * TPC + oc * C + jh * 128:b * TPC + oc * C + (jh + 1) * 128],
                                mx_q[:, b * TPC + oc * C:b * TPC + (oc + 1) * C],
                                start=(dc == 0), stop=(dc == DHH - 1))
                    ins.then_inc(*mx_at.inc())
                    tensor.wait_ge(mx_me.sem, hi + 1)
                    if hi >= 1:
                        tensor.wait_ge(mx_sb.sem, hi)
                    if hi >= 2:
                        tensor.wait_ge(mx_se.sem, hi - 1)
                    for ih in range(JH):
                        for jh in range(JH):
                            tensor.matmul(
                                mx_po[:, pb + ih * 256:pb + (ih + 1) * 256],
                                mx_adT[:, (hi % 2) * 512 + jh * 256 + ih * 128:(hi % 2) * 512 + jh * 256 + (ih + 1) * 128],
                                mx_vt[:, (b * JH + jh) * H + h * DH:(b * JH + jh) * H + (h + 1) * DH],
                                start=(jh == 0), stop=False)
                        for dc in range(DHH):
                            oc = 2 * h + dc
                            ins = tensor.matmul(
                                mx_po[:, pb + ih * 256:pb + (ih + 1) * 256],
                                mx_qg[:, b * TPC + oc * C + ih * 128:b * TPC + oc * C + (ih + 1) * 128],
                                S_bf[:, h * 512 + dc * 256:h * 512 + (dc + 1) * 256],
                                start=False, stop=(dc == DHH - 1))
                    ins.then_inc(*mx_om.inc())
                    if hi >= 2:
                        tensor.wait_ge(mx_su.sem, hi - 1)
                    for dh in range(DHH):
                        for jh in range(JH):
                            ins = tensor.matmul(
                                mx_pk[:, pb + dh * 256:pb + (dh + 1) * 256],
                                mx_ktl[:, (b * JH + jh) * H + h * DH + dh * 128:(b * JH + jh) * H + h * DH + (dh + 1) * 128],
                                mx_vt[:, (b * JH + jh) * H + h * DH:(b * JH + jh) * H + (h + 1) * DH],
                                start=(jh == 0), stop=(jh == JH - 1))
                    ins.then_inc(*mx_km.inc())

        @blk.scalar
        def _(scalar):
            for n in range(NCH):
                b = n % 2
                for h in range(HEADS):
                    hi = n * HEADS + h
                    scalar.wait_ge(mx_om.sem, hi + 1)
                    if n >= 2 and h == 0:
                        scalar.wait_ge(mx_sout[n % 2].sem, 32 * (n // 2))
                    for ih in range(JH):
                        ins = scalar.activation(
                            out=mx_s[:, (b * JH + ih) * H + h * DH:(b * JH + ih) * H + (h + 1) * DH],
                            in_=mx_po[:, (hi % 2) * 512 + ih * 256:(hi % 2) * 512 + (ih + 1) * 256],
                            func=AF.Copy)
                        if ih == JH - 1:
                            ins.then_inc(*mx_se.inc())

        @blk.gpsimd
        def _(gp):
            for n in range(NCH):
                b = n % 2
                gp.wait_ge(mx_se.sem, (n + 1) * HEADS)
                for ih in range(JH):
                    gp.dma_start(out=s_d[n * C + ih * 128:n * C + (ih + 1) * 128, :],
                                 in_=mx_s[:, (b * JH + ih) * H:(b * JH + ih + 1) * H]).then_inc(*mx_sout[b].inc(16))
            gp.wait_ge(mx_sout[0].sem, mx_sout[0].n)
            gp.wait_ge(mx_sout[1].sem, mx_sout[1].n)

    ph.close()
    if debug_stage == "mix":
        dbg_block([(dbg["s"], s_d.ap())])
        es.close()
        return nc

    ph = ExitStack()
    b2_wd = ph.enter_context(nc.sbuf_tensor("b2_wd", [128, ICH * 1024], BF16))
    b2_h = ph.enter_context(nc.sbuf_tensor("b2_h", [128, ICH * 512], BF16))
    b2_s = ph.enter_context(nc.sbuf_tensor("b2_s", [128, 4 * 1024], BF16))
    b2_x = ph.enter_context(nc.sbuf_tensor("b2_x", [128, 4 * 1024], F32))
    b2_g = ph.enter_context(nc.sbuf_tensor("b2_g", [128, 2 * 512], BF16))
    b2_sg = ph.enter_context(nc.sbuf_tensor("b2_sg", [128, 2 * 512], BF16))
    b2_o = ph.enter_context(nc.sbuf_tensor("b2_o", [128, 2 * 512], F32))
    b2_ps = ph.enter_context(nc.psum_tensor("b2_ps", [128, 4 * 512], F32))
    with nc.Block() as blk:
        b2_win = sem("b2_win")
        b2_hin = sem("b2_hin")
        b2_sx = sem("b2_sx")
        b2_mm = sem("b2_mm")
        b2_sil = sem("b2_sil")
        b2_mul = sem("b2_mul")
        b2_add = sem("b2_add")
        b2_out = [sem("b2_out0"), sem("b2_out1")]

        wd_segs = []
        for ic in range(ICH):
            g0c, g1c = 128 * ic, 128 * (ic + 1)
            g = g0c
            while g < g1c:
                csh = g // (INTER // 8)
                g1 = min(g1c, (csh + 1) * (INTER // 8))
                wd_segs.append((ic, g - g0c, g1 - g, csh, g - csh * (INTER // 8)))
                g = g1
        NSEG = len(wd_segs)

        @blk.sync
        def _(sync):
            sync.wait_ge(cc_d.sem, 1)
            for oh in range(2):
                if oh == 1:
                    sync.wait_ge(b2_mm.sem, 64)
                for (ic, p0, n, csh, il0) in wd_segs:
                    srcap = bass.AP(wdt_g, (256 * csh) * INTER + il0 * H + oh * 1024,
                                    [[H, n], [1, 1024]])
                    sync.dma_start(
                        out=b2_wd[p0:p0 + n, ic * 1024:(ic + 1) * 1024],
                        in_=srcap,
                    ).then_inc(*b2_win.inc(16))
                for tb in range(NTG):
                    it = oh * NTG + tb
                    if it >= 1:
                        sync.wait_ge(b2_mm.sem, it * 8)
                    for part in range(4):
                        sync.dma_start(
                            out=b2_h[:, part * (ICH * 128):(part + 1) * (ICH * 128)]
                                .rearrange("p (c t) -> p c t", t=128),
                            in_=h_d.ap()[:, tb * 512 + part * 128:tb * 512 + (part + 1) * 128]
                                .rearrange("(c p) t -> p c t", p=128),
                        ).then_inc(*b2_hin.inc(16))
                    if it >= 1:
                        sync.wait_ge(b2_mul.sem, it * 8)
                        sync.wait_ge(b2_add.sem, it * 8)
                    for tt in range(4):
                        sync.dma_start(out=b2_s[:, tt * 1024:(tt + 1) * 1024],
                                       in_=s_d[tb * 512 + tt * 128:tb * 512 + (tt + 1) * 128,
                                               oh * 1024:(oh + 1) * 1024]).then_inc(*b2_sx.inc(16))
                        sync.dma_start(out=b2_x[:, tt * 1024:(tt + 1) * 1024],
                                       in_=x_in[tb * 512 + tt * 128:tb * 512 + (tt + 1) * 128,
                                                oh * 1024:(oh + 1) * 1024]).then_inc(*b2_sx.inc(16))

        @blk.tensor
        def _(tensor):
            gi = 0
            for oh in range(2):
                tensor.wait_ge(b2_win.sem, 16 * NSEG * (oh + 1))
                for tb in range(NTG):
                    it = oh * NTG + tb
                    tensor.wait_ge(b2_hin.sem, 64 * (it + 1))
                    for og in range(2):
                        for tt in range(4):
                            pb = (gi % 4) * 512
                            if gi >= 4:
                                tensor.wait_ge(b2_sil.sem, gi - 3)
                            for ic in range(ICH):
                                tensor.matmul(
                                    b2_ps[:, pb:pb + 256],
                                    b2_h[:, tt * (ICH * 128) + ic * 128:tt * (ICH * 128) + (ic + 1) * 128],
                                    b2_wd[:, ic * 1024 + og * 512:ic * 1024 + og * 512 + 256],
                                    start=(ic == 0), stop=(ic == ICH - 1))
                                ins = tensor.matmul(
                                    b2_ps[:, pb + 256:pb + 512],
                                    b2_h[:, tt * (ICH * 128) + ic * 128:tt * (ICH * 128) + (ic + 1) * 128],
                                    b2_wd[:, ic * 1024 + og * 512 + 256:ic * 1024 + og * 512 + 512],
                                    start=False, stop=(ic == ICH - 1))
                            ins.then_inc(*b2_mm.inc())
                            gi += 1

        @blk.scalar
        def _(scalar):
            gi = 0
            for oh in range(2):
                for tb in range(NTG):
                    for og in range(2):
                        for tt in range(4):
                            scalar.wait_ge(b2_mm.sem, gi + 1)
                            if gi >= 2:
                                scalar.wait_ge(b2_mul.sem, gi - 1)
                            scalar.activation(out=b2_g[:, (gi % 2) * 512:(gi % 2 + 1) * 512],
                                              in_=b2_ps[:, (gi % 4) * 512:(gi % 4) * 512 + 512],
                                              func=AF.Silu).then_inc(*b2_sil.inc())
                            gi += 1

        @blk.vector
        def _(vector):
            gi = 0
            for oh in range(2):
                for tb in range(NTG):
                    it = oh * NTG + tb
                    vector.wait_ge(b2_sx.sem, 128 * (it + 1))
                    for og in range(2):
                        for tt in range(4):
                            vector.wait_ge(b2_sil.sem, gi + 1)
                            if gi >= 2:
                                vector.wait_ge(b2_add.sem, gi - 1)
                            vector.tensor_tensor(
                                b2_sg[:, (gi % 2) * 512:(gi % 2 + 1) * 512],
                                b2_g[:, (gi % 2) * 512:(gi % 2 + 1) * 512],
                                b2_s[:, tt * 1024 + og * 512:tt * 1024 + (og + 1) * 512],
                                ALU.mult).then_inc(*b2_mul.inc())
                            vector.wait_ge(b2_mul.sem, gi + 1)
                            if gi >= 2:
                                vector.wait_ge(b2_out[gi % 2].sem, 16 * (gi // 2))
                            vector.tensor_tensor(
                                b2_o[:, (gi % 2) * 512:(gi % 2 + 1) * 512],
                                b2_x[:, tt * 1024 + og * 512:tt * 1024 + (og + 1) * 512],
                                b2_sg[:, (gi % 2) * 512:(gi % 2 + 1) * 512],
                                ALU.add).then_inc(*b2_add.inc())
                            gi += 1

        @blk.gpsimd
        def _(gp):
            gi = 0
            for oh in range(2):
                for tb in range(NTG):
                    for og in range(2):
                        for tt in range(4):
                            gp.wait_ge(b2_add.sem, gi + 1)
                            gp.dma_start(
                                out=out_d[tb * 512 + tt * 128:tb * 512 + (tt + 1) * 128,
                                          oh * 1024 + og * 512:oh * 1024 + (og + 1) * 512],
                                in_=b2_o[:, (gi % 2) * 512:(gi % 2 + 1) * 512],
                            ).then_inc(*b2_out[gi % 2].inc(16))
                            gi += 1
            gp.wait_ge(b2_out[0].sem, b2_out[0].n)
            gp.wait_ge(b2_out[1].sem, b2_out[1].n)

    ph.close()
    es.close()
    return nc


def make_in_maps(L0_attn_x, L0_attn_Wq, L0_attn_bq, L0_attn_Wk, L0_attn_bk,
                 L0_attn_Wv, L0_attn_bv, L0_ffn_lut, L0_ffn_gate_idx,
                 L0_ffn_up_idx, L0_ffn_down_idx):
    cst = _consts()
    x = np.asarray(L0_attn_x)
    in_maps = []
    for c in range(NCORES):
        b, half = c >> 1, c & 1
        fs = slice(c * (H // 8), (c + 1) * (H // 8))
        isl = slice(c * (INTER // 8), (c + 1) * (INTER // 8))
        in_maps.append(dict(
            x=np.ascontiguousarray(x[b, half * TPC:(half + 1) * TPC]),
            wq=np.asarray(L0_attn_Wq),
            wk=np.asarray(L0_attn_Wk),
            wv=np.asarray(L0_attn_Wv),
            bqc=np.ascontiguousarray(np.asarray(L0_attn_bq).reshape(NFC, 128).T),
            bkc=np.ascontiguousarray(np.asarray(L0_attn_bk).reshape(NFC, 128).T),
            bvbc=np.ascontiguousarray(np.broadcast_to(np.asarray(L0_attn_bv), (128, H))),
            lutbc=np.ascontiguousarray(np.broadcast_to(np.asarray(L0_ffn_lut), (128, 16))),
            gidxT=np.ascontiguousarray(np.asarray(L0_ffn_gate_idx)[:, fs].T),
            uidxT=np.ascontiguousarray(np.asarray(L0_ffn_up_idx)[:, fs].T),
            didxT=np.ascontiguousarray(np.asarray(L0_ffn_down_idx)[:, isl].T).reshape(H // 8, INTER),
            dmask=cst["dmask"],
            crossm=cst["crossm"],
            kvs=cst["kvs"],
            ident=cst["ident"],
            ones1=cst["ones1"],
            smask=np.full((128, 1), float(half), dtype=np.float32),
        ))
    return in_maps


def kernel(**inputs):
    nc = build_program()
    in_maps = make_in_maps(**inputs)
    res = run_bass_kernel_spmd(nc, in_maps, core_ids=list(range(NCORES)))
    out = np.empty((4, 2 * TPC, H), dtype=np.float32)
    for c in range(NCORES):
        b, half = c >> 1, c & 1
        out[b, half * TPC:(half + 1) * TPC] = np.asarray(res.results[c]["out"])
    return out
